# version 33 (final)
# baseline (speedup 1.0000x reference)
import numpy as np
import ml_dtypes
from contextlib import ExitStack
import concourse.bass as bass
import concourse.mybir as mybir
from concourse.bass_utils import run_bass_kernel_spmd
from concourse.replica_groups import maybe_share_collective_output_space

F32 = mybir.dt.float32
BF16 = mybir.dt.bfloat16
AF = mybir.ActivationFunctionType
ALU = mybir.AluOpType

D = 1024
NB = 2
S = 8192
NCORE = 8
PC = S // NCORE
TOK = NB * PC
KG = D // 128
NCH = 4
CW = TOK // NCH
NS = NB * KG
HTOK = TOK // 2
NCH_H = NCH // 2
RG = [list(range(NCORE))]

ENGINES = ("sync", "gpsimd", "tensor", "scalar", "vector")

DMA_POOL = 8


class _Op:
    __slots__ = ("eng", "fn", "reads", "writes", "dma", "force", "deps",
                 "sig", "val", "waits", "sem_key", "chain", "ccop")

    def __init__(self, eng, fn, reads, writes, dma, force):
        self.eng, self.fn = eng, fn
        self.reads, self.writes = list(reads), list(writes)
        self.dma, self.force = dma, force
        self.deps = set()
        self.sig = False
        self.val = None
        self.waits = []
        self.sem_key = eng
        self.chain = None
        self.ccop = False


class Builder:

    def __init__(self):
        self.ops = []
        self.res = {}
        self.qn = {}
        self._qlist = {}
        self.last = {}
        self.fence_key = None

    def op(self, eng, fn, reads=(), writes=(), dma=False, force=False,
           fence=True):
        i = len(self.ops)
        o = _Op(eng, fn, reads, writes, dma, force)
        if fence and self.fence_key is not None:
            o.reads.append(self.fence_key)
        if dma:
            q = self.qn.get(eng, 0)
            o.sem_key = f"{eng}_d{q % DMA_POOL}"
            if q >= DMA_POOL:
                o.chain = self._qlist[eng][q - DMA_POOL]
            self._qlist.setdefault(eng, []).append(i)
            self.qn[eng] = q + 1
        for k in o.reads:
            e = self.res.get(k)
            if e:
                o.deps.update(e["w"])
        for k in o.writes:
            e = self.res.get(k)
            if e:
                o.deps.update(e["w"])
                o.deps.update(e["r"].values())
        for k in o.reads:
            self.res.setdefault(k, {"w": [], "r": {}})["r"][eng] = i
        for k in o.writes:
            self.res.setdefault(k, {"w": [], "r": {}})["w"].append(i)
        o.deps.discard(i)
        self.ops.append(o)
        self.last[eng] = i
        return i

    def barrier(self, key, lasts=None):
        deps = set((lasts or self.last).values())
        j = self.op("vector", lambda e: e.engine_nop(), writes=[key])
        self.ops[j].deps.update(d for d in deps if d != j)
        return j

    def cc(self, fn, reads=(), writes=()):
        i = self.op("gpsimd", fn, reads=reads, writes=writes, force=True)
        self.ops[i].sem_key = "cc"
        self.ops[i].ccop = True
        return i

    def finalize(self, block, semh):
        ops = self.ops
        for i, o in enumerate(ops):
            kept = {}
            for d in o.deps:
                p = ops[d]
                if p.eng == o.eng:
                    if p.eng == "tensor":
                        continue
                    if p.dma and o.dma:
                        continue
                if kept.get(p.sem_key, -1) < d:
                    kept[p.sem_key] = d
            if o.chain is not None:
                ck = ops[o.chain].sem_key
                if kept.get(ck, -1) < o.chain:
                    kept[ck] = o.chain
            o.deps = set(kept.values())
        for o in ops:
            for d in o.deps:
                ops[d].sig = True
        for o in ops:
            if o.force or o.dma:
                o.sig = True
        cnt = {}
        for o in ops:
            if o.sig:
                cnt[o.sem_key] = cnt.get(o.sem_key, 0) + (16 if o.dma else 1)
                o.val = cnt[o.sem_key]
        for o in ops:
            w = {}
            for d in o.deps:
                p = ops[d]
                w[p.sem_key] = max(w.get(p.sem_key, 0), p.val)
            o.waits = sorted(w.items())
        per_eng = {e: [o for o in ops if o.eng == e] for e in ENGINES}
        final_pool = {}
        for e in ENGINES:
            vals = {}
            for o in per_eng[e]:
                if o.dma and o.sig:
                    vals[o.sem_key] = max(vals.get(o.sem_key, 0), o.val)
            final_pool[e] = sorted(vals.items())

        def make_body(ename):
            def body(eng):
                clock = {}
                for o in per_eng[ename]:
                    for sname, val in o.waits:
                        if clock.get(sname, 0) < val:
                            eng.wait_ge(semh[sname], val)
                            clock[sname] = val
                    inst = o.fn(eng)
                    if o.sig:
                        if o.ccop:
                            inst.then_inc(semh[o.sem_key])
                        else:
                            inst.then_inc(semh[o.sem_key], 16 if o.dma else 1)
                for sname, val in final_pool[ename]:
                    if clock.get(sname, 0) < val:
                        eng.wait_ge(semh[sname], val)
                        clock[sname] = val
            return body

        for e in ENGINES:
            getattr(block, e)(make_body(e))



def build_fused():
    nc = bass.Bass("TRN2", target_bir_lowering=False, debug=False)
    xT = nc.dram_tensor("xT", [D, TOK], F32, kind="ExternalInput").ap()
    xTm = nc.dram_tensor("xTm", [D, TOK], F32, kind="ExternalInput").ap()
    wsrc = [nc.dram_tensor(nm, [D, D], F32, kind="ExternalInput").ap()
            for nm in ("wz_f", "wh_f", "wz_b", "wh_b")]
    bcolsA = nc.dram_tensor("bcolsA", [128, 4 * KG], F32, kind="ExternalInput").ap()
    w_out = nc.dram_tensor("w_out", [2 * D, D], F32, kind="ExternalInput").ap()
    fw = {nm: nc.dram_tensor(nm, [D, D], F32, kind="ExternalInput").ap()
          for nm in ("wfT", "ufT", "wiT", "uiT", "wcT", "ucT", "woT", "uoT")}
    bcolsB = nc.dram_tensor("bcolsB", [128, 6 * KG], F32, kind="ExternalInput").ap()
    maskc = nc.dram_tensor("maskc", [128, 2 * NCORE], F32, kind="ExternalInput").ap()
    o_out = nc.dram_tensor("o_out", [D, TOK], F32, kind="ExternalOutput").ap()
    pl = {("h", "f"): nc.dram_tensor("pl_hf", [D, TOK], BF16).ap(),
          ("p", "f"): nc.dram_tensor("pl_pf", [D, TOK], BF16).ap(),
          ("h", "b"): nc.dram_tensor("pl_hb", [D, TOK], BF16).ap(),
          ("p", "b"): nc.dram_tensor("pl_pb", [D, TOK], BF16).ap()}
    cc_in = nc.dram_tensor("cc_in", [128, 4 * NS], F32)
    cc_out = nc.dram_tensor(
        "cc_out", [NCORE * 128, 4 * NS], F32,
        addr_space=maybe_share_collective_output_space(
            "AllGather", RG, prefer_local=True))

    B = Builder()
    PAIRS = [("f", 0), ("i", 0), ("c", 0), ("f", 1), ("i", 1), ("c", 1), ("o", None)]
    WMAT = {"f": ("wfT", "ufT"), "i": ("wiT", "uiT"), "c": ("wcT", "ucT"),
            "o": ("woT", "uoT")}

    with ExitStack() as st:
        ec = st.enter_context
        xnT = ec(nc.sbuf_tensor("xnT", [128, KG * TOK], BF16))
        ccs = ec(nc.sbuf_tensor("ccs", [128, 4 * NS], F32))
        wmatA = ec(nc.sbuf_tensor("wmatA", [128, 2 * KG * D], BF16))
        semh = {"sync": ec(nc.semaphore("s_sync")), "gpsimd": ec(nc.semaphore("s_gp")),
                "tensor": ec(nc.semaphore("s_pe")), "scalar": ec(nc.semaphore("s_act")),
                "vector": ec(nc.semaphore("s_ve")), "cc": ec(nc.semaphore("s_cc"))}
        for q in ("sync", "gpsimd", "scalar"):
            for r in range(DMA_POOL):
                semh[f"{q}_d{r}"] = ec(nc.semaphore(f"s_{q}_d{r}"))
        block = ec(nc.Block())

        stA = ExitStack()
        eca = stA.enter_context
        bc = eca(nc.sbuf_tensor("bc", [128, 4 * KG], F32))
        bzneg = eca(nc.sbuf_tensor("bzneg", [128, 2 * KG], F32))
        bh05 = eca(nc.sbuf_tensor("bh05", [128, 2 * KG], F32))
        onesc = eca(nc.sbuf_tensor("onesc", [128, 1], F32))
        ones1 = eca(nc.sbuf_tensor("ones1", [1, 128], F32))
        epsc = eca(nc.sbuf_tensor("epsc", [128, 1], F32))
        wmatB = eca(nc.sbuf_tensor("wmatB", [128, KG * D], BF16))
        xnTm = eca(nc.sbuf_tensor("xnTm", [128, KG * TOK], BF16))
        psk = eca(nc.psum_tensor("psk", [128, TOK], F32))
        psu = eca(nc.psum_tensor("psu", [128, TOK], F32))

        def wtile(sl, kg, m):
            if sl == 2:
                return wmatB[:, kg * D + m * 128: kg * D + (m + 1) * 128]
            return wmatA[:, (sl * KG + kg) * D + m * 128:
                         (sl * KG + kg) * D + (m + 1) * 128]

        def load_gate(sl, wi):
            for kg in range(KG):
                dst = (wmatB[:, kg * D:(kg + 1) * D] if sl == 2 else
                       wmatA[:, (sl * KG + kg) * D:(sl * KG + kg + 1) * D])
                B.op("gpsimd", (lambda dst, kg, wi: lambda e: e.dma_start(
                    out=dst, in_=wsrc[wi][kg * 128:(kg + 1) * 128, :]))(dst, kg, wi),
                    writes=[f"wsl{sl}"], dma=True)

        B.op("sync", lambda e: e.dma_start(out=bc[:], in_=bcolsA[:, :]),
             writes=["bc"], dma=True)
        load_gate(0, 2)
        load_gate(1, 3)
        load_gate(2, 0)
        B.op("gpsimd", lambda e: e.memset(onesc[:], 1.0), writes=["onesc"])
        B.op("gpsimd", lambda e: e.memset(ones1[:], 1.0), writes=["ones1"])
        B.op("gpsimd", lambda e: e.memset(epsc[:], 1e-5), writes=["epsc"])
        B.op("vector", lambda e: e.tensor_scalar_mul(bzneg[:, 0:KG], bc[:, 0:KG], -1.0),
             reads=["bc"], writes=["bzneg"])
        B.op("vector", lambda e: e.tensor_scalar_mul(
            bzneg[:, KG:2 * KG], bc[:, 2 * KG:3 * KG], -1.0),
            reads=["bc"], writes=["bzneg"])
        B.op("vector", lambda e: e.tensor_scalar_add(bh05[:, 0:KG], bc[:, KG:2 * KG], 0.5),
             reads=["bc"], writes=["bh05"])
        B.op("vector", lambda e: e.tensor_scalar_add(
            bh05[:, KG:2 * KG], bc[:, 3 * KG:4 * KG], 0.5),
            reads=["bc"], writes=["bh05"])

        stLN = ExitStack()
        xfull2 = stLN.enter_context(nc.sbuf_tensor("xfull2", [128, 2 * KG * HTOK], F32))
        scr = stLN.enter_context(nc.sbuf_tensor("scr", [128, 3 * CW], F32))
        rows = stLN.enter_context(nc.sbuf_tensor("rows", [1, 6 * HTOK], F32))

        def _xf(lp):
            return xfull2[:, (lp % 2) * KG * HTOK:((lp % 2) + 1) * KG * HTOK]

        def _xk(lp, kg):
            return f"xf{kg}p{lp % 2}"

        def r01(i, lp, tcl=None):
            o = (2 * (lp % 2) + i) * HTOK
            if tcl is None:
                return rows[0:1, o:o + HTOK]
            return rows[0:1, o + tcl * CW: o + (tcl + 1) * CW]

        def r23(j, tcl=None):
            o = (4 + j) * HTOK
            if tcl is None:
                return rows[0:1, o:o + HTOK]
            return rows[0:1, o + tcl * CW: o + (tcl + 1) * CW]

        def ln_loads(xsrc, half, lp):
            xfullh = _xf(lp)
            for kg in range(KG):
                B.op("sync" if kg % 2 == 0 else "scalar",
                     (lambda kg, xfullh: lambda e: e.dma_start(
                         out=xfullh[:, kg * HTOK:(kg + 1) * HTOK],
                         in_=xsrc[kg * 128:(kg + 1) * 128,
                                  half * HTOK:(half + 1) * HTOK]))(kg, xfullh),
                     writes=[_xk(lp, kg)], dma=True)

        def ln_stats(lp):
            xfullh = _xf(lp)
            ps_s1 = psu[0:1, 0:CW]
            ps_s2 = psu[0:1, CW:2 * CW]
            for tcl in range(NCH_H):
                for kg in range(KG):
                    sp = scr[:, (kg % 2) * CW:(kg % 2 + 1) * CW]
                    xsl = xfullh[:, kg * HTOK + tcl * CW: kg * HTOK + (tcl + 1) * CW]
                    B.op("vector", (lambda sp, xsl: lambda e: e.tensor_mul(
                        sp, xsl, xsl))(sp, xsl),
                        reads=[_xk(lp, kg)], writes=[f"scr{kg % 2}"])
                    B.op("tensor", (lambda xsl, kg: lambda e: e.matmul(
                        ps_s1, onesc[:, 0:1], xsl,
                        start=(kg == 0), stop=(kg == KG - 1)))(xsl, kg),
                        reads=[_xk(lp, kg), "onesc"], writes=["psu_s"])
                    B.op("tensor", (lambda sp, kg: lambda e: e.matmul(
                        ps_s2, onesc[:, 0:1], sp,
                        start=(kg == 0), stop=(kg == KG - 1)))(sp, kg),
                        reads=[f"scr{kg % 2}", "onesc"], writes=["psu_s"])
                B.op("vector", (lambda lp, tcl: lambda e: e.tensor_copy(
                    r01(0, lp, tcl), psu[0:1, 0:CW]))(lp, tcl),
                    reads=["psu_s"], writes=[f"row0p{lp % 2}"])
                B.op("vector", (lambda lp, tcl: lambda e: e.tensor_copy(
                    r01(1, lp, tcl), psu[0:1, CW:2 * CW]))(lp, tcl),
                    reads=["psu_s"], writes=[f"row1p{lp % 2}"])

        def ln_finish(dst, pfx, half, lp):
            pk = lp % 2
            B.op("vector", (lambda lp: lambda e: e.tensor_scalar_mul(
                r01(0, lp), r01(0, lp), 1.0 / D))(lp),
                reads=[f"row0p{pk}"], writes=[f"row0p{pk}"])
            B.op("vector", (lambda lp: lambda e: e.tensor_mul(
                r23(0), r01(0, lp), r01(0, lp)))(lp),
                reads=[f"row0p{pk}"], writes=["row2", "row2b"])
            B.op("vector", (lambda lp: lambda e: e.scalar_tensor_tensor(
                r23(1), r01(1, lp), 1.0 / D, r23(0),
                op0=ALU.mult, op1=ALU.subtract))(lp),
                reads=[f"row1p{pk}", "row2"], writes=["row3"])
            B.op("scalar", lambda e: e.activation(r23(1), r23(1), AF.Ln,
                                                  bias=epsc[0:1, 0:1]),
                 reads=["row3", "epsc"], writes=["row3"])
            B.op("scalar", lambda e: e.activation(r23(1), r23(1), AF.Exp, scale=-0.5),
                 reads=["row3"], writes=["row3"])
            B.op("vector", (lambda lp: lambda e: e.scalar_tensor_tensor(
                r23(0), r01(0, lp), -1.0, r23(1), op0=ALU.mult, op1=ALU.mult))(lp),
                reads=[f"row0p{pk}", "row3"], writes=["row2b"])
            xfullh = _xf(lp)
            psr_b = psk[:, 0:CW]
            psm_b = psk[:, CW:2 * CW]
            sapp = scr[:, 2 * CW:3 * CW]
            for tcl in range(NCH_H):
                tc = half * NCH_H + tcl
                B.op("tensor", (lambda tcl: lambda e: e.matmul(
                    psr_b, ones1[0:1, :], r23(1, tcl), start=True, stop=True))(tcl),
                    reads=["row3", "ones1"], writes=["psk_b"])
                B.op("tensor", (lambda tcl: lambda e: e.matmul(
                    psm_b, ones1[0:1, :], r23(0, tcl), start=True, stop=True))(tcl),
                    reads=["row2b", "ones1"], writes=["psk_b"])
                for kg in range(KG):
                    ci = kg * HTOK + tcl * CW
                    co = kg * TOK + tc * CW
                    B.op("vector", (lambda ci, xfullh: lambda e: e.tensor_mul(
                        sapp, xfullh[:, ci:ci + CW], psr_b))(ci, xfullh),
                        reads=[_xk(lp, kg), "psk_b"], writes=["sapp"])
                    B.op("vector", (lambda co, kg, tc: lambda e: e.tensor_add(
                        dst[:, co:co + CW], sapp, psm_b))(co, kg, tc),
                        reads=["sapp", "psk_b"], writes=[f"{pfx}{kg}t{tc}"])

        LNP = [(xTm, xnTm, "xm", 0), (xTm, xnTm, "xm", 1),
               (xT, xnT, "xn", 0), (xT, xnT, "xn", 1)]
        ln_loads(LNP[0][0], LNP[0][3], 0)
        ln_loads(LNP[1][0], LNP[1][3], 1)
        ln_stats(0)
        ln_finish(LNP[0][1], LNP[0][2], LNP[0][3], 0)
        ln_loads(LNP[2][0], LNP[2][3], 2)
        ln_stats(1)
        ln_finish(LNP[1][1], LNP[1][2], LNP[1][3], 1)
        ln_loads(LNP[3][0], LNP[3][3], 3)
        ln_stats(2)
        ln_finish(LNP[2][1], LNP[2][2], LNP[2][3], 2)
        ln_stats(3)
        ln_finish(LNP[3][1], LNP[3][2], LNP[3][3], 3)
        stLN.close()

        B.barrier("fA")
        B.fence_key = "fA"

        zrs = eca(nc.sbuf_tensor("zrs", [128, PC], F32))
        abuf = eca(nc.sbuf_tensor("abuf", [128, 2 * TOK], F32))
        bbuf = eca(nc.sbuf_tensor("bbuf", [128, 2 * TOK], F32))
        zbuf = eca(nc.sbuf_tensor("zbuf", [128, TOK], F32))
        sgbuf = eca(nc.sbuf_tensor("sgbuf", [128, TOK], F32))
        tbuf = eca(nc.sbuf_tensor("tbuf", [128, TOK], F32))
        hsc = eca(nc.sbuf_tensor("hsc", [128, 2 * TOK], F32))
        psc = eca(nc.sbuf_tensor("psc", [128, 2 * TOK], F32))
        B.op("gpsimd", lambda e: e.memset(zrs[:], 0.0), writes=["zrs"])

        def scan_phase(dnm, di, xn_sb, xpfx, slK, slU, ccoff,
                       after_last_psk=None, after_last_psu=None):
            for m in range(KG):
                par = m % 2
                for sl, psq, pk in ((slK, psk, "psk"), (slU, psu, "psu")):
                    extra = ["psk_b"] if pk == "psk" else ["psu_s"]
                    for tc2 in range(NCH):
                        for kg in range(KG):
                            rds = [f"{xpfx}{kg}t{tc2}"] + ([f"wsl{sl}"] if kg == 0 else [])
                            B.op("tensor", (lambda sl, kg, m, tc2, psq: lambda e: e.matmul(
                                psq[:, tc2 * CW:(tc2 + 1) * CW],
                                wtile(sl, kg, m),
                                xn_sb[:, kg * TOK + tc2 * CW: kg * TOK + (tc2 + 1) * CW],
                                start=(kg == 0), stop=(kg == KG - 1)))(sl, kg, m, tc2, psq),
                                reads=rds,
                                writes=[pk] + (extra if kg == 0 and tc2 < 2 and m == 0 else []),
                                fence=False)
                    if m == KG - 1:
                        hook = after_last_psk if pk == "psk" else after_last_psu
                        if hook is not None:
                            hook()
                ab = abuf[:, par * TOK:(par + 1) * TOK]
                bb = bbuf[:, par * TOK:(par + 1) * TOK]
                hs = hsc[:, par * TOK:(par + 1) * TOK]
                pscm = psc[:, par * TOK:(par + 1) * TOK]
                bzn = bzneg[:, di * KG + m: di * KG + m + 1]
                bzp = bc[:, 2 * di * KG + m: 2 * di * KG + m + 1]
                bhp = bc[:, (2 * di + 1) * KG + m: (2 * di + 1) * KG + m + 1]
                bh5 = bh05[:, di * KG + m: di * KG + m + 1]
                B.op("scalar", (lambda ab, bzn: lambda e: e.activation(
                    ab, psk[:, :], AF.Sigmoid, bias=bzn, scale=-1.0))(ab, bzn),
                    reads=["psk", "bzneg"], writes=[f"ab{par}"])
                B.op("vector", (lambda ab: lambda e: e.tensor_scalar(
                    zbuf[:], ab, -1.0, 1.0, op0=ALU.mult, op1=ALU.add))(ab),
                    reads=[f"ab{par}"], writes=["zbuf"])
                B.op("scalar", (lambda bhp: lambda e: e.activation(
                    sgbuf[:], psu[:, :], AF.Sigmoid, bias=bhp))(bhp),
                    reads=["psu", "bc"], writes=["sgbuf"])
                B.op("vector", (lambda bh5: lambda e: e.scalar_tensor_tensor(
                    tbuf[:], psu[:, :], bh5, sgbuf[:],
                    op0=ALU.add, op1=ALU.max))(bh5),
                    reads=["psu", "bh05", "sgbuf"], writes=["tbuf"])
                B.op("vector", (lambda bb: lambda e: e.tensor_mul(
                    bb, zbuf[:], tbuf[:]))(bb),
                    reads=["zbuf", "tbuf"], writes=[f"bb{par}"])
                for b in range(NB):
                    cs = slice(b * PC, (b + 1) * PC)
                    if dnm == "f":
                        a_ap, b_ap = ab[:, cs], bb[:, cs]
                    else:
                        a_ap, b_ap = ab[:, cs][:, ::-1], bb[:, cs][:, ::-1]
                    B.op("vector", (lambda hs, cs, a_ap, b_ap: lambda e: e.tensor_tensor_scan(
                        hs[:, cs], a_ap, b_ap, 0.0, op0=ALU.mult, op1=ALU.add))(hs, cs, a_ap, b_ap),
                        reads=[f"ab{par}", f"bb{par}"], writes=[f"hs{par}"])
                    B.op("vector", (lambda pscm, cs, a_ap: lambda e: e.tensor_tensor_scan(
                        pscm[:, cs], a_ap, zrs[:, :], 1.0,
                        op0=ALU.mult, op1=ALU.add))(pscm, cs, a_ap),
                        reads=[f"ab{par}", "zrs"], writes=[f"ps{par}"])
                B.op("vector", (lambda hs, m: lambda e: e.tensor_copy(
                    ccs[:, ccoff + m: ccoff + 2 * KG: KG],
                    hs[:, PC - 1::PC]))(hs, m),
                    reads=[f"hs{par}"], writes=["ccs"])
                B.op("vector", (lambda pscm, m: lambda e: e.tensor_copy(
                    ccs[:, ccoff + NS + m: ccoff + NS + 2 * KG: KG],
                    pscm[:, PC - 1::PC]))(pscm, m),
                    reads=[f"ps{par}"], writes=["ccs"])
                B.op("gpsimd", (lambda dnm, m, hs: lambda e: e.dma_start(
                    out=pl[("h", dnm)][m * 128:(m + 1) * 128, :], in_=hs))(dnm, m, hs),
                    reads=[f"hs{par}"], writes=[f"d_h{dnm}"], dma=True, force=True)
                B.op("gpsimd", (lambda dnm, m, pscm: lambda e: e.dma_start(
                    out=pl[("p", dnm)][m * 128:(m + 1) * 128, :], in_=pscm))(dnm, m, pscm),
                    reads=[f"ps{par}"], writes=[f"d_p{dnm}"], dma=True, force=True)

        def _ld_wmatA(nm, half, keys):
            for kg in range(KG):
                B.op("gpsimd", (lambda nm, half, kg: lambda e: e.dma_start(
                    out=wmatA[:, ((1 - half) * KG + kg) * D:
                              ((1 - half) * KG + kg + 1) * D],
                    in_=fw[nm][kg * 128:(kg + 1) * 128, :]))(nm, half, kg),
                    writes=keys, dma=True)

        scan_phase("b", 1, xnTm, "xm", 0, 1, 2 * NS,
                   after_last_psk=lambda: load_gate(0, 1),
                   after_last_psu=lambda: _ld_wmatA("wfT", 0, ["wqb0w", "wsl1"]))
        scan_phase("f", 0, xnT, "xn", 2, 0, 0,
                   after_last_psu=lambda: _ld_wmatA("ufT", 1, ["wqb0u", "wsl0"]))

        B.op("sync", lambda e: e.dma_start(out=cc_in[:, :], in_=ccs[:]),
             reads=["ccs"], writes=["d_ccin"], dma=True)
        lastsnap = dict(B.last)
        B.cc(lambda e: e.collective_compute(
            "AllGather", ALU.bypass, replica_groups=RG,
            ins=[cc_in.ap().opt()], outs=[cc_out.ap().opt()]),
            reads=["d_ccin"], writes=["d_ccout"])

        stA.close()

        B.barrier("fB", lasts=lastsnap)
        B.fence_key = "fB"

        stB = ExitStack()
        ecb = stB.enter_context
        bcs = ecb(nc.sbuf_tensor("bcs", [128, 6 * KG], F32))
        msk = ecb(nc.sbuf_tensor("msk", [128, 2 * NCORE], F32))
        sums = ecb(nc.sbuf_tensor("sums", [128, NCORE * 4 * NS], BF16))
        sel = ecb(nc.sbuf_tensor("sel", [128, 2 * NS], F32))
        pref = ecb(nc.sbuf_tensor("pref", [128, 18 * NS], F32))
        cbf = ecb(nc.sbuf_tensor("cbf", [128, KG * PC], BF16))
        hfc = ecb(nc.sbuf_tensor("hfc", [128, 2 * KG * CW], BF16))
        hbc = ecb(nc.sbuf_tensor("hbc", [128, 2 * KG * CW], BF16))
        stg = ecb(nc.sbuf_tensor("stg", [128, 4 * CW], BF16))
        psg = ecb(nc.psum_tensor("psg", [128, 4 * PC], F32))
        WQ = [None, None]

        def wqkey(li, half):
            if li % 2 == 0:
                return "wqb0w" if half == 0 else "wqb0u"
            return "wqb1"

        def load_pair(li):
            q, _t = PAIRS[li]
            wn, un = WMAT[q]
            for half, nm in ((0, wn), (1, un)):
                for kg in range(KG):
                    if li % 2 == 0:
                        dst = wmatA[:, ((1 - half) * KG + kg) * D:
                                    ((1 - half) * KG + kg + 1) * D]
                    else:
                        dst = WQ[1][:, (half * KG + kg) * D:(half * KG + kg + 1) * D]
                    B.op("gpsimd", (lambda dst, kg, nm: lambda e: e.dma_start(
                        out=dst, in_=fw[nm][kg * 128:(kg + 1) * 128, :]))(dst, kg, nm),
                        writes=[wqkey(li, half)], dma=True)

        def ptile(li, kg, half, m):
            if li % 2 == 0:
                off = ((1 - half) * KG + kg) * D
                return wmatA[:, off + m * 128: off + (m + 1) * 128]
            off = (half * KG + kg) * D
            return WQ[1][:, off + m * 128: off + (m + 1) * 128]

        B.op("sync", lambda e: e.dma_start(out=bcs[:], in_=bcolsB[:, :]),
             writes=["bcs"], dma=True)
        B.op("sync", lambda e: e.dma_start(out=msk[:], in_=maskc[:, :]),
             writes=["msk"], dma=True)
        def emit_fold():
            for r in range(NCORE):
                B.op("gpsimd", (lambda r: lambda e: e.dma_start(
                    out=sums[:, r * 4 * NS:(r + 1) * 4 * NS],
                    in_=cc_out[r * 128:(r + 1) * 128, :]))(r),
                    reads=["d_ccout"], writes=["sums"], dma=True)
            B.op("vector", lambda e: e.memset(pref[:, 0:NS], 0.0), writes=["pref0"])
            B.op("vector", lambda e: e.memset(pref[:, 9 * NS:10 * NS], 0.0), writes=["pref9"])
            for half in range(2):
                ho, po, pb0 = (0, NS, 0) if half == 0 else (2 * NS, 3 * NS, 9)
                for r in range(NCORE):
                    base = r * 4 * NS
                    mc = msk[:, r:r + 1]
                    imc = msk[:, NCORE + r:NCORE + r + 1]
                    src = pref[:, (pb0 + r) * NS: (pb0 + r + 1) * NS]
                    dst = pref[:, (pb0 + r + 1) * NS: (pb0 + r + 2) * NS]
                    B.op("vector", (lambda base, po, mc, imc: lambda e: e.tensor_scalar(
                        sel[:, NS:2 * NS], sums[:, base + po: base + po + NS],
                        mc, imc, op0=ALU.mult, op1=ALU.add))(base, po, mc, imc),
                        reads=["sums", "msk"], writes=["selp"])
                    B.op("vector", (lambda base, ho, mc: lambda e: e.tensor_scalar(
                        sel[:, 0:NS], sums[:, base + ho: base + ho + NS],
                        mc, None, op0=ALU.mult))(base, ho, mc),
                        reads=["sums", "msk"], writes=["selh"])
                    B.op("vector", (lambda dst, src: lambda e: e.tensor_mul(
                        dst, sel[:, NS:2 * NS], src))(dst, src),
                        reads=["selp", f"pref{pb0 + r}"], writes=[f"pref{pb0 + r + 1}"])
                    B.op("vector", (lambda dst: lambda e: e.tensor_add(
                        dst, dst, sel[:, 0:NS]))(dst),
                        reads=["selh", f"pref{pb0 + r + 1}"], writes=[f"pref{pb0 + r + 1}"])

        xnreads = [f"xn{kg}t{tc}" for kg in range(KG) for tc in range(NCH)]
        cbreads = [f"cb{kg}" for kg in range(KG)]

        TCS = (0, 2, 1, 3)

        def emit_chunk_loads(tci):
            tc = TCS[tci]
            par = tci % 2
            b = tc // 2
            for dirn, (hk, pk2, dst, dkey, prefkey, prefblk) in enumerate((
                    (("h", "f"), ("p", "f"), hfc, "hfc", "pref8", 8),
                    (("h", "b"), ("p", "b"), hbc, "hbc", "pref17", 17))):
                B.op("sync", (lambda hk, tc, par, dst: lambda e: e.dma_start(
                    out=dst.ap().rearrange("p (q kg c) -> p q kg c", q=2, c=CW)[:, par],
                    in_=pl[hk][:, tc * CW:(tc + 1) * CW].rearrange(
                        "(kg p) c -> p kg c", p=128)))(hk, tc, par, dst),
                    reads=[f"d_h{hk[1]}"],
                    writes=[f"{dkey}L{par}", f"{dkey}{par}"], dma=True)
                for kh in range(2):
                    B.op("sync", (lambda pk2, kh, tc: lambda e: e.dma_start(
                        out=stg.ap().rearrange("p (kg c) -> p kg c", c=CW),
                        in_=pl[pk2][kh * 512:(kh + 1) * 512, tc * CW:(tc + 1) * CW].rearrange(
                            "(kg p) c -> p kg c", p=128)))(pk2, kh, tc),
                        reads=[f"d_p{pk2[1]}"],
                        writes=["stg"], dma=True)
                    for kg in range(kh * 4, kh * 4 + 4):
                        pcol = pref[:, prefblk * NS + b * KG + kg: prefblk * NS + b * KG + kg + 1]
                        B.op("vector", (lambda dst, par, kg, pcol: lambda e: e.scalar_tensor_tensor(
                            dst[:, par * KG * CW + kg * CW: par * KG * CW + (kg + 1) * CW],
                            stg[:, (kg % 4) * CW:(kg % 4 + 1) * CW], pcol,
                            dst[:, par * KG * CW + kg * CW: par * KG * CW + (kg + 1) * CW],
                            op0=ALU.mult, op1=ALU.add))(dst, par, kg, pcol),
                            reads=["stg", f"{dkey}L{par}", prefkey],
                            writes=[f"{dkey}{par}"])

        wqB = ecb(nc.sbuf_tensor("wqB", [128, 2 * KG * D], BF16))
        ph2 = ExitStack()
        cst = ph2.enter_context(nc.sbuf_tensor("cst", [128, KG * PC], F32))
        gpl = ph2.enter_context(nc.sbuf_tensor("gpl", [128, KG * PC], BF16))
        gcm2 = ph2.enter_context(nc.sbuf_tensor("gcm2", [128, 2 * PC], F32))
        if True:
            WQ[1] = wqB
            load_pair(1)
            for m in range(KG):
                icol = bcs[:, 5 * KG + m: 5 * KG + m + 1]
                B.op("vector", (lambda m, icol: lambda e: e.tensor_scalar(
                    cst[:, m * PC:(m + 1) * PC], xnT[:, 0:PC], 0.0, icol,
                    op0=ALU.mult, op1=ALU.add))(m, icol),
                    reads=["bcs", "xn0t0", "xn0t1"], writes=[f"c{m}"])
                B.op("vector", (lambda m, icol: lambda e: e.tensor_scalar(
                    cbf[:, m * PC:(m + 1) * PC], xnT[:, 0:PC], 0.0, icol,
                    op0=ALU.mult, op1=ALU.add))(m, icol),
                    reads=["bcs", "xn0t0", "xn0t1"], writes=[f"cb{m}"])

            for t in range(2):
                for qi, q in enumerate(("f", "i", "c")):
                    li = t * 3 + qi
                    for mb in (0, 4):
                        first_blk = (li == 0 and mb == 0)
                        for m in range(mb, mb + 4):
                            slot = m % 4
                            for nh in range(2):
                                c0 = slot * PC + nh * CW
                                bk = [f"pb{c0 // CW}"]
                                for kg in range(KG):
                                    rds = ([wqkey(li, 0)] + xnreads
                                           if kg == 0 else
                                           ([wqkey(li, 0)] if kg == KG - 1 else []))
                                    xb = (["psk", "psu", "psk_b", "psu_s"]
                                          if first_blk and m == mb and nh == 0 and kg == 0
                                          else [])
                                    B.op("tensor", (lambda li, kg, m, t, nh, c0: lambda e: e.matmul(
                                        psg[:, c0:c0 + CW],
                                        ptile(li, kg, 0, m),
                                        xnT[:, kg * TOK + t * PC + nh * CW: kg * TOK + t * PC + (nh + 1) * CW],
                                        start=(kg == 0), stop=False))(li, kg, m, t, nh, c0),
                                        reads=rds, writes=bk + xb,
                                        fence=not first_blk)
                        for m in range(mb, mb + 4):
                            slot = m % 4
                            for nh in range(2):
                                c0 = slot * PC + nh * CW
                                bk = [f"pb{c0 // CW}"]
                                for kg in range(KG):
                                    rds = [f"cb{kg}"] + ([wqkey(li, 1)] if kg in (0, KG - 1) else [])
                                    B.op("tensor", (lambda li, kg, m, nh, c0: lambda e: e.matmul(
                                        psg[:, c0:c0 + CW],
                                        ptile(li, kg, 1, m),
                                        cbf[:, kg * PC + nh * CW: kg * PC + (nh + 1) * CW],
                                        start=False, stop=(kg == KG - 1)))(li, kg, m, nh, c0),
                                        reads=rds, writes=bk)
                            fn = AF.Tanh if q == "c" else AF.Sigmoid
                            bq = bcs[:, qi * KG + m: qi * KG + m + 1]
                            pbk = [f"pb{2 * slot}", f"pb{2 * slot + 1}"]
                            if q != "c":
                                B.op("scalar", (lambda m, slot, fn, bq: lambda e: e.activation(
                                    gpl[:, m * PC:(m + 1) * PC],
                                    psg[:, slot * PC:(slot + 1) * PC], fn, bias=bq))(m, slot, fn, bq),
                                    reads=pbk + ["bcs"], writes=[f"gpl{m}"])
                            else:
                                par2 = m % 2
                                gsl = gcm2[:, par2 * PC:(par2 + 1) * PC]
                                B.op("scalar", (lambda gsl, slot, bq: lambda e: e.activation(
                                    gsl, psg[:, slot * PC:(slot + 1) * PC],
                                    AF.Tanh, bias=bq))(gsl, slot, bq),
                                    reads=pbk + ["bcs"], writes=[f"gc{par2}"])
                                B.op("vector", (lambda gsl, m: lambda e: e.tensor_mul(
                                    gsl, gpl[:, m * PC:(m + 1) * PC], gsl))(gsl, m),
                                    reads=[f"gpl{m}", f"gc{par2}"], writes=[f"gc{par2}"])
                                B.op("vector", (lambda gsl, m: lambda e: e.tensor_add(
                                    cst[:, m * PC:(m + 1) * PC], cst[:, m * PC:(m + 1) * PC],
                                    gsl))(gsl, m),
                                    reads=[f"gc{par2}", f"c{m}"], writes=[f"c{m}"])
                    if q == "c":
                        for m in range(KG):
                            B.op("vector", (lambda m: lambda e: e.tensor_copy(
                                cbf[:, m * PC:(m + 1) * PC],
                                cst[:, m * PC:(m + 1) * PC]))(m),
                                reads=[f"c{m}"], writes=[f"cb{m}"])
                    if q == "f":
                        for m in range(KG):
                            B.op("vector", (lambda m: lambda e: e.tensor_mul(
                                cst[:, m * PC:(m + 1) * PC], cst[:, m * PC:(m + 1) * PC],
                                gpl[:, m * PC:(m + 1) * PC]))(m),
                                reads=[f"gpl{m}", f"c{m}"], writes=[f"c{m}"])
                    if li + 2 <= 6:
                        load_pair(li + 2)
                    if li == 0:
                        emit_fold()
                    elif li == 2:
                        emit_chunk_loads(0)
                    elif li == 4:
                        emit_chunk_loads(1)

            ffn_keys = ([f"gpl{m}" for m in range(KG)] + [f"c{m}" for m in range(KG)]
                        + [f"cb{m}" for m in range(KG)] + ["gc0", "gc1"])
            B.op("vector", lambda e: e.engine_nop(), reads=ffn_keys,
                 writes=["ffn_done"])

        ph2.close()
        xch = ecb(nc.sbuf_tensor("xch", [128, 2 * CW], F32))
        sch = ecb(nc.sbuf_tensor("sch", [128, KG * CW], F32))
        uoc = ecb(nc.sbuf_tensor("uoc", [128, KG * CW], F32))
        hf2 = ecb(nc.sbuf_tensor("hf2", [128, 2 * CW], F32))
        sg2 = ecb(nc.sbuf_tensor("sg2", [128, 2 * CW], F32))
        g2 = ecb(nc.sbuf_tensor("g2", [128, 2 * CW], F32))
        p2 = ecb(nc.sbuf_tensor("p2", [128, 2 * CW], F32))
        och = ecb(nc.sbuf_tensor("och", [128, 2 * CW], F32))
        if True:
            FD = ["ffn_done"]
            for kg in range(2 * KG):
                B.op("gpsimd", (lambda kg: lambda e: e.dma_start(
                    out=wqB[:, kg * D:(kg + 1) * D],
                    in_=w_out[kg * 128:(kg + 1) * 128, :]))(kg),
                    writes=[f"wout{kg}"] + (["wqb1"] if kg == 0 else []),
                    dma=True)

            for tci, tc in enumerate(TCS):
                par = tci % 2
                b = tc // 2
                ppos = tc - 2 * b
                if tci >= 1 and tci + 1 < NCH:
                    emit_chunk_loads(tci + 1)
                if tci % 2 == 0:
                    for m in range(KG):
                        slot = m % 4
                        fc0 = 4 * CW + slot * CW
                        for kg in range(KG):
                            rds = ["wqb0u"] + cbreads if kg == 0 else (["wqb0u"] if kg == KG - 1 else [])
                            B.op("tensor", (lambda kg, m, ppos, fc0: lambda e: e.matmul(
                                psg[:, fc0:fc0 + CW],
                                ptile(6, kg, 1, m),
                                cbf[:, kg * PC + ppos * CW: kg * PC + (ppos + 1) * CW],
                                start=(kg == 0), stop=(kg == KG - 1)))(kg, m, ppos, fc0),
                                reads=rds, writes=[f"pb{4 + slot}"])
                        B.op("scalar", (lambda m, fc0: lambda e: e.activation(
                            uoc[:, m * CW:(m + 1) * CW], psg[:, fc0:fc0 + CW],
                            AF.Copy))(m, fc0),
                            reads=FD + [f"pb{4 + (m % 4)}"], writes=[f"uoc{m}"])
                for m in range(KG):
                    slot = m % 4
                    for kgc in range(2 * KG):
                        if kgc < KG:
                            mv = hfc[:, par * KG * CW + kgc * CW: par * KG * CW + (kgc + 1) * CW]
                        else:
                            mv = hbc[:, par * KG * CW + (kgc - KG) * CW: par * KG * CW + (kgc - KG + 1) * CW]
                        rds = ([f"wout{kgc}"]
                               + ([f"hfc{par}", f"hbc{par}"]
                                  if kgc in (0, 2 * KG - 1) else []))
                        B.op("tensor", (lambda kgc, m, slot, mv: lambda e: e.matmul(
                            psg[:, slot * CW:(slot + 1) * CW],
                            wqB[:, kgc * D + m * 128: kgc * D + (m + 1) * 128],
                            mv, start=(kgc == 0), stop=(kgc == 2 * KG - 1)))(kgc, m, slot, mv),
                            reads=rds, writes=[f"pb{slot}"])
                    bocol = bcs[:, 4 * KG + m: 4 * KG + m + 1]
                    B.op("vector", (lambda m, slot, bocol: lambda e: e.tensor_scalar(
                        sch[:, m * CW:(m + 1) * CW], psg[:, slot * CW:(slot + 1) * CW],
                        bocol, None, op0=ALU.add))(m, slot, bocol),
                        reads=FD + [f"pb{slot}", "bcs"], writes=[f"sch{m}"])
                for m in range(KG):
                    slot = m % 4
                    mp = m % 2
                    fc0 = 4 * CW + slot * CW
                    B.op("scalar", (lambda m, tc, mp: lambda e: e.dma_start(
                        out=xch[:, mp * CW:(mp + 1) * CW],
                        in_=xT[m * 128:(m + 1) * 128, tc * CW:(tc + 1) * CW]))(m, tc, mp),
                        reads=FD, writes=[f"xch{mp}"], dma=True)
                    for kg in range(KG):
                        rds = ["wqb0w"] + xnreads if kg == 0 else (["wqb0w"] if kg == KG - 1 else [])
                        B.op("tensor", (lambda kg, m, tc, fc0: lambda e: e.matmul(
                            psg[:, fc0:fc0 + CW],
                            ptile(6, kg, 0, m),
                            xnT[:, kg * TOK + tc * CW: kg * TOK + (tc + 1) * CW],
                            start=(kg == 0), stop=(kg == KG - 1)))(kg, m, tc, fc0),
                            reads=rds, writes=[f"pb{4 + slot}"])
                    B.op("vector", (lambda mp, fc0, m: lambda e: e.tensor_add(
                        hf2[:, mp * CW:(mp + 1) * CW], psg[:, fc0:fc0 + CW],
                        uoc[:, m * CW:(m + 1) * CW]))(mp, fc0, m),
                        reads=FD + [f"pb{4 + slot}", f"uoc{m}"], writes=[f"hf2{mp}"])
                    bfcol = bcs[:, 3 * KG + m: 3 * KG + m + 1]
                    B.op("scalar", (lambda mp, bfcol: lambda e: e.activation(
                        hf2[:, mp * CW:(mp + 1) * CW], hf2[:, mp * CW:(mp + 1) * CW],
                        AF.Tanh, bias=bfcol))(mp, bfcol),
                        reads=[f"hf2{mp}", "bcs"], writes=[f"hf2{mp}"])
                    B.op("scalar", (lambda mp: lambda e: e.activation(
                        sg2[:, mp * CW:(mp + 1) * CW], hf2[:, mp * CW:(mp + 1) * CW],
                        AF.Sigmoid))(mp),
                        reads=FD + [f"hf2{mp}"], writes=[f"sg2{mp}"])
                    B.op("vector", (lambda mp: lambda e: e.tensor_mul(
                        g2[:, mp * CW:(mp + 1) * CW], hf2[:, mp * CW:(mp + 1) * CW],
                        sg2[:, mp * CW:(mp + 1) * CW]))(mp),
                        reads=FD + [f"hf2{mp}", f"sg2{mp}"], writes=[f"g2{mp}"])
                    B.op("vector", (lambda m, mp: lambda e: e.tensor_mul(
                        p2[:, mp * CW:(mp + 1) * CW], sch[:, m * CW:(m + 1) * CW],
                        g2[:, mp * CW:(mp + 1) * CW]))(m, mp),
                        reads=FD + [f"g2{mp}", f"sch{m}"], writes=[f"p2{mp}"])
                    oq = m % 2
                    B.op("vector", (lambda m, mp, oq: lambda e: e.tensor_add(
                        och[:, oq * CW:(oq + 1) * CW], p2[:, mp * CW:(mp + 1) * CW],
                        xch[:, mp * CW:(mp + 1) * CW]))(m, mp, oq),
                        reads=FD + [f"p2{mp}", f"xch{mp}"], writes=[f"och{oq}"])
                    B.op("sync", (lambda m, tc, oq: lambda e: e.dma_start(
                        out=o_out[m * 128:(m + 1) * 128, tc * CW:(tc + 1) * CW],
                        in_=och[:, oq * CW:(oq + 1) * CW]))(m, tc, oq),
                        reads=[f"och{oq}"], dma=True, force=True)

        B.finalize(block, semh)
        stB.close()
    return nc



def _bias_cols(*vecs):
    cols = [np.ascontiguousarray(np.asarray(v, np.float32).reshape(KG, 128).T)
            for v in vecs]
    return np.ascontiguousarray(np.concatenate(cols, axis=1))


def kernel(**inputs):
    f32 = np.float32
    x = np.asarray(inputs["L0_attn_x"], f32)
    g = lambda k: np.asarray(inputs[k], f32)

    xT_shards = []
    for c in range(NCORE):
        sl = x[:, c * PC:(c + 1) * PC, :]
        xT_shards.append(np.ascontiguousarray(sl.transpose(2, 0, 1).reshape(D, TOK)))

    bcols_A = _bias_cols(g("L0_attn_bz_f"), g("L0_attn_bh_f"),
                         g("L0_attn_bz_b"), g("L0_attn_bh_b"))
    wA = {nm: np.ascontiguousarray(g(src)) for nm, src in
          (("wz_f", "L0_attn_Wz_f"), ("wh_f", "L0_attn_Wh_f"),
           ("wz_b", "L0_attn_Wz_b"), ("wh_b", "L0_attn_Wh_b"))}
    bcols_B = _bias_cols(g("L0_ffn_bf"), g("L0_ffn_bi"), g("L0_ffn_bc"),
                         g("L0_ffn_bo"), g("L0_attn_b_out"), g("L0_ffn_in_c"))
    w_out = np.ascontiguousarray(g("L0_attn_W_out"))
    fwt = {nm + "T": np.ascontiguousarray(g("L0_ffn_" + nm).T)
           for nm in ("wf", "uf", "wi", "ui", "wc", "uc", "wo", "uo")}

    masks = []
    for c in range(NCORE):
        m = np.zeros((128, 2 * NCORE), f32)
        m[:, :c] = 1.0
        m[:, NCORE:NCORE + NCORE] = 1.0 - m[:, :NCORE]
        masks.append(np.ascontiguousarray(m))

    nc = build_fused()
    in_maps = [dict(xT=xT_shards[c], xTm=xT_shards[NCORE - 1 - c],
                    bcolsA=bcols_A, bcolsB=bcols_B, maskc=masks[c],
                    w_out=w_out, **wA, **fwt)
               for c in range(NCORE)]
    res = run_bass_kernel_spmd(nc, in_maps, core_ids=list(range(NCORE)))

    out = np.empty((NB, S, D), f32)
    for c in range(NCORE):
        oc = np.asarray(res.results[c]["o_out"])
        out[:, c * PC:(c + 1) * PC, :] = oc.reshape(D, NB, PC).transpose(1, 2, 0)
    return out


# version 34
# speedup vs baseline: 1.0183x; 1.0183x over previous
import numpy as np
import ml_dtypes
from contextlib import ExitStack
import concourse.bass as bass
import concourse.mybir as mybir
from concourse.bass_utils import run_bass_kernel_spmd
from concourse.replica_groups import maybe_share_collective_output_space

F32 = mybir.dt.float32
BF16 = mybir.dt.bfloat16
AF = mybir.ActivationFunctionType
ALU = mybir.AluOpType

D = 1024
NB = 2
S = 8192
NCORE = 8
PC = S // NCORE
TOK = NB * PC
KG = D // 128
NCH = 4
CW = TOK // NCH
NS = NB * KG
HTOK = TOK // 2
NCH_H = NCH // 2
RG = [list(range(NCORE))]

ENGINES = ("sync", "gpsimd", "tensor", "scalar", "vector")

DMA_POOL = 8


class _Op:
    __slots__ = ("eng", "fn", "reads", "writes", "dma", "force", "deps",
                 "sig", "val", "waits", "sem_key", "chain", "ccop")

    def __init__(self, eng, fn, reads, writes, dma, force):
        self.eng, self.fn = eng, fn
        self.reads, self.writes = list(reads), list(writes)
        self.dma, self.force = dma, force
        self.deps = set()
        self.sig = False
        self.val = None
        self.waits = []
        self.sem_key = eng
        self.chain = None
        self.ccop = False


class Builder:

    def __init__(self):
        self.ops = []
        self.res = {}
        self.qn = {}
        self._qlist = {}
        self.last = {}
        self.fence_key = None

    def op(self, eng, fn, reads=(), writes=(), dma=False, force=False,
           fence=True):
        i = len(self.ops)
        o = _Op(eng, fn, reads, writes, dma, force)
        if fence and self.fence_key is not None:
            o.reads.append(self.fence_key)
        if dma:
            q = self.qn.get(eng, 0)
            o.sem_key = f"{eng}_d{q % DMA_POOL}"
            if q >= DMA_POOL:
                o.chain = self._qlist[eng][q - DMA_POOL]
            self._qlist.setdefault(eng, []).append(i)
            self.qn[eng] = q + 1
        for k in o.reads:
            e = self.res.get(k)
            if e:
                o.deps.update(e["w"])
        for k in o.writes:
            e = self.res.get(k)
            if e:
                o.deps.update(e["w"])
                o.deps.update(e["r"].values())
        for k in o.reads:
            self.res.setdefault(k, {"w": [], "r": {}})["r"][eng] = i
        for k in o.writes:
            self.res.setdefault(k, {"w": [], "r": {}})["w"].append(i)
        o.deps.discard(i)
        self.ops.append(o)
        self.last[eng] = i
        return i

    def barrier(self, key, lasts=None):
        deps = set((lasts or self.last).values())
        j = self.op("vector", lambda e: e.engine_nop(), writes=[key])
        self.ops[j].deps.update(d for d in deps if d != j)
        return j

    def cc(self, fn, reads=(), writes=()):
        i = self.op("gpsimd", fn, reads=reads, writes=writes, force=True)
        self.ops[i].sem_key = "cc"
        self.ops[i].ccop = True
        return i

    def finalize(self, block, semh):
        ops = self.ops
        for i, o in enumerate(ops):
            kept = {}
            for d in o.deps:
                p = ops[d]
                if p.eng == o.eng:
                    if p.eng == "tensor":
                        continue
                    if p.dma and o.dma:
                        continue
                if kept.get(p.sem_key, -1) < d:
                    kept[p.sem_key] = d
            if o.chain is not None:
                ck = ops[o.chain].sem_key
                if kept.get(ck, -1) < o.chain:
                    kept[ck] = o.chain
            o.deps = set(kept.values())
        for o in ops:
            for d in o.deps:
                ops[d].sig = True
        for o in ops:
            if o.force or o.dma:
                o.sig = True
        cnt = {}
        for o in ops:
            if o.sig:
                cnt[o.sem_key] = cnt.get(o.sem_key, 0) + (16 if o.dma else 1)
                o.val = cnt[o.sem_key]
        for o in ops:
            w = {}
            for d in o.deps:
                p = ops[d]
                w[p.sem_key] = max(w.get(p.sem_key, 0), p.val)
            o.waits = sorted(w.items())
        per_eng = {e: [o for o in ops if o.eng == e] for e in ENGINES}
        final_pool = {}
        for e in ENGINES:
            vals = {}
            for o in per_eng[e]:
                if o.dma and o.sig:
                    vals[o.sem_key] = max(vals.get(o.sem_key, 0), o.val)
            final_pool[e] = sorted(vals.items())

        def make_body(ename):
            def body(eng):
                clock = {}
                for o in per_eng[ename]:
                    for sname, val in o.waits:
                        if clock.get(sname, 0) < val:
                            eng.wait_ge(semh[sname], val)
                            clock[sname] = val
                    inst = o.fn(eng)
                    if o.sig:
                        if o.ccop:
                            inst.then_inc(semh[o.sem_key])
                        else:
                            inst.then_inc(semh[o.sem_key], 16 if o.dma else 1)
                for sname, val in final_pool[ename]:
                    if clock.get(sname, 0) < val:
                        eng.wait_ge(semh[sname], val)
                        clock[sname] = val
            return body

        for e in ENGINES:
            getattr(block, e)(make_body(e))



def build_fused():
    nc = bass.Bass("TRN2", target_bir_lowering=False, debug=False)
    xT = nc.dram_tensor("xT", [D, TOK], F32, kind="ExternalInput").ap()
    xTm = nc.dram_tensor("xTm", [D, TOK], F32, kind="ExternalInput").ap()
    wsrc = [nc.dram_tensor(nm, [D, D], F32, kind="ExternalInput").ap()
            for nm in ("wz_f", "wh_f", "wz_b", "wh_b")]
    bcolsA = nc.dram_tensor("bcolsA", [128, 4 * KG], F32, kind="ExternalInput").ap()
    w_out = nc.dram_tensor("w_out", [2 * D, D], F32, kind="ExternalInput").ap()
    fw = {nm: nc.dram_tensor(nm, [D, D], F32, kind="ExternalInput").ap()
          for nm in ("wfT", "ufT", "wiT", "uiT", "wcT", "ucT", "woT", "uoT")}
    bcolsB = nc.dram_tensor("bcolsB", [128, 6 * KG], F32, kind="ExternalInput").ap()
    maskc = nc.dram_tensor("maskc", [128, 2 * NCORE], F32, kind="ExternalInput").ap()
    o_out = nc.dram_tensor("o_out", [D, TOK], F32, kind="ExternalOutput").ap()
    pl = {("h", "f"): nc.dram_tensor("pl_hf", [NCH * 128, KG * CW], BF16).ap(),
          ("p", "f"): nc.dram_tensor("pl_pf", [NCH * 128, KG * CW], BF16).ap(),
          ("h", "b"): nc.dram_tensor("pl_hb", [NCH * 128, KG * CW], BF16).ap(),
          ("p", "b"): nc.dram_tensor("pl_pb", [NCH * 128, KG * CW], BF16).ap()}
    cc_in = nc.dram_tensor("cc_in", [128, 4 * NS], F32)
    cc_out = nc.dram_tensor(
        "cc_out", [NCORE * 128, 4 * NS], F32,
        addr_space=maybe_share_collective_output_space(
            "AllGather", RG, prefer_local=True))

    B = Builder()
    PAIRS = [("f", 0), ("i", 0), ("c", 0), ("f", 1), ("i", 1), ("c", 1), ("o", None)]
    WMAT = {"f": ("wfT", "ufT"), "i": ("wiT", "uiT"), "c": ("wcT", "ucT"),
            "o": ("woT", "uoT")}

    with ExitStack() as st:
        ec = st.enter_context
        xnT = ec(nc.sbuf_tensor("xnT", [128, KG * TOK], BF16))
        ccs = ec(nc.sbuf_tensor("ccs", [128, 4 * NS], F32))
        wmatA = ec(nc.sbuf_tensor("wmatA", [128, 2 * KG * D], BF16))
        semh = {"sync": ec(nc.semaphore("s_sync")), "gpsimd": ec(nc.semaphore("s_gp")),
                "tensor": ec(nc.semaphore("s_pe")), "scalar": ec(nc.semaphore("s_act")),
                "vector": ec(nc.semaphore("s_ve")), "cc": ec(nc.semaphore("s_cc"))}
        for q in ("sync", "gpsimd", "scalar"):
            for r in range(DMA_POOL):
                semh[f"{q}_d{r}"] = ec(nc.semaphore(f"s_{q}_d{r}"))
        block = ec(nc.Block())

        stA = ExitStack()
        eca = stA.enter_context
        bc = eca(nc.sbuf_tensor("bc", [128, 4 * KG], F32))
        bzneg = eca(nc.sbuf_tensor("bzneg", [128, 2 * KG], F32))
        bh05 = eca(nc.sbuf_tensor("bh05", [128, 2 * KG], F32))
        onesc = eca(nc.sbuf_tensor("onesc", [128, 1], F32))
        ones1 = eca(nc.sbuf_tensor("ones1", [1, 128], F32))
        epsc = eca(nc.sbuf_tensor("epsc", [128, 1], F32))
        wmatB = eca(nc.sbuf_tensor("wmatB", [128, KG * D], BF16))
        xnTm = eca(nc.sbuf_tensor("xnTm", [128, KG * TOK], BF16))
        psk = eca(nc.psum_tensor("psk", [128, TOK], F32))
        psu = eca(nc.psum_tensor("psu", [128, TOK], F32))

        def wtile(sl, kg, m):
            if sl == 2:
                return wmatB[:, kg * D + m * 128: kg * D + (m + 1) * 128]
            return wmatA[:, (sl * KG + kg) * D + m * 128:
                         (sl * KG + kg) * D + (m + 1) * 128]

        def load_gate(sl, wi):
            for kg in range(KG):
                dst = (wmatB[:, kg * D:(kg + 1) * D] if sl == 2 else
                       wmatA[:, (sl * KG + kg) * D:(sl * KG + kg + 1) * D])
                B.op("gpsimd", (lambda dst, kg, wi: lambda e: e.dma_start(
                    out=dst, in_=wsrc[wi][kg * 128:(kg + 1) * 128, :]))(dst, kg, wi),
                    writes=[f"wsl{sl}"], dma=True)

        B.op("sync", lambda e: e.dma_start(out=bc[:], in_=bcolsA[:, :]),
             writes=["bc"], dma=True)
        load_gate(0, 2)
        load_gate(1, 3)
        load_gate(2, 0)
        B.op("gpsimd", lambda e: e.memset(onesc[:], 1.0), writes=["onesc"])
        B.op("gpsimd", lambda e: e.memset(ones1[:], 1.0), writes=["ones1"])
        B.op("gpsimd", lambda e: e.memset(epsc[:], 1e-5), writes=["epsc"])
        B.op("vector", lambda e: e.tensor_scalar_mul(bzneg[:, 0:KG], bc[:, 0:KG], -1.0),
             reads=["bc"], writes=["bzneg"])
        B.op("vector", lambda e: e.tensor_scalar_mul(
            bzneg[:, KG:2 * KG], bc[:, 2 * KG:3 * KG], -1.0),
            reads=["bc"], writes=["bzneg"])
        B.op("vector", lambda e: e.tensor_scalar_add(bh05[:, 0:KG], bc[:, KG:2 * KG], 0.5),
             reads=["bc"], writes=["bh05"])
        B.op("vector", lambda e: e.tensor_scalar_add(
            bh05[:, KG:2 * KG], bc[:, 3 * KG:4 * KG], 0.5),
            reads=["bc"], writes=["bh05"])

        stLN = ExitStack()
        xfull2 = stLN.enter_context(nc.sbuf_tensor("xfull2", [128, 2 * KG * HTOK], F32))
        scr = stLN.enter_context(nc.sbuf_tensor("scr", [128, 3 * CW], F32))
        rows = stLN.enter_context(nc.sbuf_tensor("rows", [1, 6 * HTOK], F32))

        def _xf(lp):
            return xfull2[:, (lp % 2) * KG * HTOK:((lp % 2) + 1) * KG * HTOK]

        def _xk(lp, kg):
            return f"xf{kg}p{lp % 2}"

        def r01(i, lp, tcl=None):
            o = (2 * (lp % 2) + i) * HTOK
            if tcl is None:
                return rows[0:1, o:o + HTOK]
            return rows[0:1, o + tcl * CW: o + (tcl + 1) * CW]

        def r23(j, tcl=None):
            o = (4 + j) * HTOK
            if tcl is None:
                return rows[0:1, o:o + HTOK]
            return rows[0:1, o + tcl * CW: o + (tcl + 1) * CW]

        def ln_loads(xsrc, half, lp):
            xfullh = _xf(lp)
            for kg in range(KG):
                B.op("sync" if kg % 2 == 0 else "scalar",
                     (lambda kg, xfullh: lambda e: e.dma_start(
                         out=xfullh[:, kg * HTOK:(kg + 1) * HTOK],
                         in_=xsrc[kg * 128:(kg + 1) * 128,
                                  half * HTOK:(half + 1) * HTOK]))(kg, xfullh),
                     writes=[_xk(lp, kg)], dma=True)

        def ln_stats(lp):
            xfullh = _xf(lp)
            ps_s1 = psu[0:1, 0:CW]
            ps_s2 = psu[0:1, CW:2 * CW]
            for tcl in range(NCH_H):
                for kg in range(KG):
                    sp = scr[:, (kg % 2) * CW:(kg % 2 + 1) * CW]
                    xsl = xfullh[:, kg * HTOK + tcl * CW: kg * HTOK + (tcl + 1) * CW]
                    B.op("vector", (lambda sp, xsl: lambda e: e.tensor_mul(
                        sp, xsl, xsl))(sp, xsl),
                        reads=[_xk(lp, kg)], writes=[f"scr{kg % 2}"])
                    B.op("tensor", (lambda xsl, kg: lambda e: e.matmul(
                        ps_s1, onesc[:, 0:1], xsl,
                        start=(kg == 0), stop=(kg == KG - 1)))(xsl, kg),
                        reads=[_xk(lp, kg), "onesc"], writes=["psu_s"])
                    B.op("tensor", (lambda sp, kg: lambda e: e.matmul(
                        ps_s2, onesc[:, 0:1], sp,
                        start=(kg == 0), stop=(kg == KG - 1)))(sp, kg),
                        reads=[f"scr{kg % 2}", "onesc"], writes=["psu_s"])
                B.op("vector", (lambda lp, tcl: lambda e: e.tensor_copy(
                    r01(0, lp, tcl), psu[0:1, 0:CW]))(lp, tcl),
                    reads=["psu_s"], writes=[f"row0p{lp % 2}"])
                B.op("vector", (lambda lp, tcl: lambda e: e.tensor_copy(
                    r01(1, lp, tcl), psu[0:1, CW:2 * CW]))(lp, tcl),
                    reads=["psu_s"], writes=[f"row1p{lp % 2}"])

        def ln_finish(dst, pfx, half, lp):
            pk = lp % 2
            B.op("vector", (lambda lp: lambda e: e.tensor_scalar_mul(
                r01(0, lp), r01(0, lp), 1.0 / D))(lp),
                reads=[f"row0p{pk}"], writes=[f"row0p{pk}"])
            B.op("vector", (lambda lp: lambda e: e.tensor_mul(
                r23(0), r01(0, lp), r01(0, lp)))(lp),
                reads=[f"row0p{pk}"], writes=["row2", "row2b"])
            B.op("vector", (lambda lp: lambda e: e.scalar_tensor_tensor(
                r23(1), r01(1, lp), 1.0 / D, r23(0),
                op0=ALU.mult, op1=ALU.subtract))(lp),
                reads=[f"row1p{pk}", "row2"], writes=["row3"])
            B.op("scalar", lambda e: e.activation(r23(1), r23(1), AF.Ln,
                                                  bias=epsc[0:1, 0:1]),
                 reads=["row3", "epsc"], writes=["row3"])
            B.op("scalar", lambda e: e.activation(r23(1), r23(1), AF.Exp, scale=-0.5),
                 reads=["row3"], writes=["row3"])
            B.op("vector", (lambda lp: lambda e: e.scalar_tensor_tensor(
                r23(0), r01(0, lp), -1.0, r23(1), op0=ALU.mult, op1=ALU.mult))(lp),
                reads=[f"row0p{pk}", "row3"], writes=["row2b"])
            xfullh = _xf(lp)
            psr_b = psk[:, 0:CW]
            psm_b = psk[:, CW:2 * CW]
            sapp = scr[:, 2 * CW:3 * CW]
            for tcl in range(NCH_H):
                tc = half * NCH_H + tcl
                B.op("tensor", (lambda tcl: lambda e: e.matmul(
                    psr_b, ones1[0:1, :], r23(1, tcl), start=True, stop=True))(tcl),
                    reads=["row3", "ones1"], writes=["psk_b"])
                B.op("tensor", (lambda tcl: lambda e: e.matmul(
                    psm_b, ones1[0:1, :], r23(0, tcl), start=True, stop=True))(tcl),
                    reads=["row2b", "ones1"], writes=["psk_b"])
                for kg in range(KG):
                    ci = kg * HTOK + tcl * CW
                    co = kg * TOK + tc * CW
                    B.op("vector", (lambda ci, xfullh: lambda e: e.tensor_mul(
                        sapp, xfullh[:, ci:ci + CW], psr_b))(ci, xfullh),
                        reads=[_xk(lp, kg), "psk_b"], writes=["sapp"])
                    B.op("vector", (lambda co, kg, tc: lambda e: e.tensor_add(
                        dst[:, co:co + CW], sapp, psm_b))(co, kg, tc),
                        reads=["sapp", "psk_b"], writes=[f"{pfx}{kg}t{tc}"])

        LNP = [(xTm, xnTm, "xm", 0), (xTm, xnTm, "xm", 1),
               (xT, xnT, "xn", 0), (xT, xnT, "xn", 1)]
        ln_loads(LNP[0][0], LNP[0][3], 0)
        ln_loads(LNP[1][0], LNP[1][3], 1)
        ln_stats(0)
        ln_finish(LNP[0][1], LNP[0][2], LNP[0][3], 0)
        ln_loads(LNP[2][0], LNP[2][3], 2)
        ln_stats(1)
        ln_finish(LNP[1][1], LNP[1][2], LNP[1][3], 1)
        ln_loads(LNP[3][0], LNP[3][3], 3)
        ln_stats(2)
        ln_finish(LNP[2][1], LNP[2][2], LNP[2][3], 2)
        ln_stats(3)
        ln_finish(LNP[3][1], LNP[3][2], LNP[3][3], 3)
        stLN.close()

        B.barrier("fA")
        B.fence_key = "fA"

        zrs = eca(nc.sbuf_tensor("zrs", [128, PC], F32))
        abuf = eca(nc.sbuf_tensor("abuf", [128, 2 * TOK], F32))
        bbuf = eca(nc.sbuf_tensor("bbuf", [128, 2 * TOK], F32))
        zbuf = eca(nc.sbuf_tensor("zbuf", [128, TOK], F32))
        sgbuf = eca(nc.sbuf_tensor("sgbuf", [128, TOK], F32))
        tbuf = eca(nc.sbuf_tensor("tbuf", [128, TOK], F32))
        hsc = eca(nc.sbuf_tensor("hsc", [128, 2 * TOK], F32))
        psc = eca(nc.sbuf_tensor("psc", [128, 2 * TOK], F32))
        B.op("gpsimd", lambda e: e.memset(zrs[:], 0.0), writes=["zrs"])

        def scan_phase(dnm, di, xn_sb, xpfx, slK, slU, ccoff,
                       after_last_psk=None, after_last_psu=None):
            for m in range(KG):
                par = m % 2
                for sl, psq, pk in ((slK, psk, "psk"), (slU, psu, "psu")):
                    extra = ["psk_b"] if pk == "psk" else ["psu_s"]
                    for tc2 in range(NCH):
                        for kg in range(KG):
                            rds = [f"{xpfx}{kg}t{tc2}"] + ([f"wsl{sl}"] if kg == 0 else [])
                            B.op("tensor", (lambda sl, kg, m, tc2, psq: lambda e: e.matmul(
                                psq[:, tc2 * CW:(tc2 + 1) * CW],
                                wtile(sl, kg, m),
                                xn_sb[:, kg * TOK + tc2 * CW: kg * TOK + (tc2 + 1) * CW],
                                start=(kg == 0), stop=(kg == KG - 1)))(sl, kg, m, tc2, psq),
                                reads=rds,
                                writes=[pk] + (extra if kg == 0 and tc2 < 2 and m == 0 else []),
                                fence=False)
                    if m == KG - 1:
                        hook = after_last_psk if pk == "psk" else after_last_psu
                        if hook is not None:
                            hook()
                ab = abuf[:, par * TOK:(par + 1) * TOK]
                bb = bbuf[:, par * TOK:(par + 1) * TOK]
                hs = hsc[:, par * TOK:(par + 1) * TOK]
                pscm = psc[:, par * TOK:(par + 1) * TOK]
                bzn = bzneg[:, di * KG + m: di * KG + m + 1]
                bzp = bc[:, 2 * di * KG + m: 2 * di * KG + m + 1]
                bhp = bc[:, (2 * di + 1) * KG + m: (2 * di + 1) * KG + m + 1]
                bh5 = bh05[:, di * KG + m: di * KG + m + 1]
                B.op("scalar", (lambda ab, bzn: lambda e: e.activation(
                    ab, psk[:, :], AF.Sigmoid, bias=bzn, scale=-1.0))(ab, bzn),
                    reads=["psk", "bzneg"], writes=[f"ab{par}"])
                B.op("vector", (lambda ab: lambda e: e.tensor_scalar(
                    zbuf[:], ab, -1.0, 1.0, op0=ALU.mult, op1=ALU.add))(ab),
                    reads=[f"ab{par}"], writes=["zbuf"])
                B.op("scalar", (lambda bhp: lambda e: e.activation(
                    sgbuf[:], psu[:, :], AF.Sigmoid, bias=bhp))(bhp),
                    reads=["psu", "bc"], writes=["sgbuf"])
                B.op("vector", (lambda bh5: lambda e: e.scalar_tensor_tensor(
                    tbuf[:], psu[:, :], bh5, sgbuf[:],
                    op0=ALU.add, op1=ALU.max))(bh5),
                    reads=["psu", "bh05", "sgbuf"], writes=["tbuf"])
                B.op("vector", (lambda bb: lambda e: e.tensor_mul(
                    bb, zbuf[:], tbuf[:]))(bb),
                    reads=["zbuf", "tbuf"], writes=[f"bb{par}"])
                for b in range(NB):
                    cs = slice(b * PC, (b + 1) * PC)
                    if dnm == "f":
                        a_ap, b_ap = ab[:, cs], bb[:, cs]
                    else:
                        a_ap, b_ap = ab[:, cs][:, ::-1], bb[:, cs][:, ::-1]
                    B.op("vector", (lambda hs, cs, a_ap, b_ap: lambda e: e.tensor_tensor_scan(
                        hs[:, cs], a_ap, b_ap, 0.0, op0=ALU.mult, op1=ALU.add))(hs, cs, a_ap, b_ap),
                        reads=[f"ab{par}", f"bb{par}"], writes=[f"hs{par}"])
                    B.op("vector", (lambda pscm, cs, a_ap: lambda e: e.tensor_tensor_scan(
                        pscm[:, cs], a_ap, zrs[:, :], 1.0,
                        op0=ALU.mult, op1=ALU.add))(pscm, cs, a_ap),
                        reads=[f"ab{par}", "zrs"], writes=[f"ps{par}"])
                B.op("vector", (lambda hs, m: lambda e: e.tensor_copy(
                    ccs[:, ccoff + m: ccoff + 2 * KG: KG],
                    hs[:, PC - 1::PC]))(hs, m),
                    reads=[f"hs{par}"], writes=["ccs"])
                B.op("vector", (lambda pscm, m: lambda e: e.tensor_copy(
                    ccs[:, ccoff + NS + m: ccoff + NS + 2 * KG: KG],
                    pscm[:, PC - 1::PC]))(pscm, m),
                    reads=[f"ps{par}"], writes=["ccs"])
                B.op("gpsimd", (lambda dnm, m, hs: lambda e: e.dma_start(
                    out=pl[("h", dnm)][:, m * CW:(m + 1) * CW].rearrange(
                        "(tc p) c -> p tc c", p=128),
                    in_=hs.rearrange("p (tc c) -> p tc c", c=CW)))(dnm, m, hs),
                    reads=[f"hs{par}"], writes=[f"d_h{dnm}"], dma=True, force=True)
                B.op("gpsimd", (lambda dnm, m, pscm: lambda e: e.dma_start(
                    out=pl[("p", dnm)][:, m * CW:(m + 1) * CW].rearrange(
                        "(tc p) c -> p tc c", p=128),
                    in_=pscm.rearrange("p (tc c) -> p tc c", c=CW)))(dnm, m, pscm),
                    reads=[f"ps{par}"], writes=[f"d_p{dnm}"], dma=True, force=True)

        def _ld_wmatA(nm, half, keys):
            for kg in range(KG):
                B.op("gpsimd", (lambda nm, half, kg: lambda e: e.dma_start(
                    out=wmatA[:, ((1 - half) * KG + kg) * D:
                              ((1 - half) * KG + kg + 1) * D],
                    in_=fw[nm][kg * 128:(kg + 1) * 128, :]))(nm, half, kg),
                    writes=keys, dma=True)

        scan_phase("b", 1, xnTm, "xm", 0, 1, 2 * NS,
                   after_last_psk=lambda: load_gate(0, 1),
                   after_last_psu=lambda: _ld_wmatA("wfT", 0, ["wqb0w", "wsl1"]))
        scan_phase("f", 0, xnT, "xn", 2, 0, 0,
                   after_last_psu=lambda: _ld_wmatA("ufT", 1, ["wqb0u", "wsl0"]))

        B.op("sync", lambda e: e.dma_start(out=cc_in[:, :], in_=ccs[:]),
             reads=["ccs"], writes=["d_ccin"], dma=True)
        lastsnap = dict(B.last)
        B.cc(lambda e: e.collective_compute(
            "AllGather", ALU.bypass, replica_groups=RG,
            ins=[cc_in.ap().opt()], outs=[cc_out.ap().opt()]),
            reads=["d_ccin"], writes=["d_ccout"])

        stA.close()

        B.barrier("fB", lasts=lastsnap)
        B.fence_key = "fB"

        stB = ExitStack()
        ecb = stB.enter_context
        bcs = ecb(nc.sbuf_tensor("bcs", [128, 6 * KG], F32))
        msk = ecb(nc.sbuf_tensor("msk", [128, 2 * NCORE], F32))
        sums = ecb(nc.sbuf_tensor("sums", [128, NCORE * 4 * NS], BF16))
        sel = ecb(nc.sbuf_tensor("sel", [128, 2 * NS], F32))
        pref = ecb(nc.sbuf_tensor("pref", [128, 18 * NS], F32))
        cbf = ecb(nc.sbuf_tensor("cbf", [128, KG * PC], BF16))
        hfc = ecb(nc.sbuf_tensor("hfc", [128, 2 * KG * CW], BF16))
        hbc = ecb(nc.sbuf_tensor("hbc", [128, 2 * KG * CW], BF16))
        stg = ecb(nc.sbuf_tensor("stg", [128, 4 * CW], BF16))
        psg = ecb(nc.psum_tensor("psg", [128, 4 * PC], F32))
        WQ = [None, None]

        def wqkey(li, half):
            if li % 2 == 0:
                return "wqb0w" if half == 0 else "wqb0u"
            return "wqb1"

        def load_pair(li):
            q, _t = PAIRS[li]
            wn, un = WMAT[q]
            for half, nm in ((0, wn), (1, un)):
                for kg in range(KG):
                    if li % 2 == 0:
                        dst = wmatA[:, ((1 - half) * KG + kg) * D:
                                    ((1 - half) * KG + kg + 1) * D]
                    else:
                        dst = WQ[1][:, (half * KG + kg) * D:(half * KG + kg + 1) * D]
                    B.op("gpsimd", (lambda dst, kg, nm: lambda e: e.dma_start(
                        out=dst, in_=fw[nm][kg * 128:(kg + 1) * 128, :]))(dst, kg, nm),
                        writes=[wqkey(li, half)], dma=True)

        def ptile(li, kg, half, m):
            if li % 2 == 0:
                off = ((1 - half) * KG + kg) * D
                return wmatA[:, off + m * 128: off + (m + 1) * 128]
            off = (half * KG + kg) * D
            return WQ[1][:, off + m * 128: off + (m + 1) * 128]

        B.op("sync", lambda e: e.dma_start(out=bcs[:], in_=bcolsB[:, :]),
             writes=["bcs"], dma=True)
        B.op("sync", lambda e: e.dma_start(out=msk[:], in_=maskc[:, :]),
             writes=["msk"], dma=True)
        def emit_fold():
            for r in range(NCORE):
                B.op("gpsimd", (lambda r: lambda e: e.dma_start(
                    out=sums[:, r * 4 * NS:(r + 1) * 4 * NS],
                    in_=cc_out[r * 128:(r + 1) * 128, :]))(r),
                    reads=["d_ccout"], writes=["sums"], dma=True)
            B.op("vector", lambda e: e.memset(pref[:, 0:NS], 0.0), writes=["pref0"])
            B.op("vector", lambda e: e.memset(pref[:, 9 * NS:10 * NS], 0.0), writes=["pref9"])
            for half in range(2):
                ho, po, pb0 = (0, NS, 0) if half == 0 else (2 * NS, 3 * NS, 9)
                for r in range(NCORE):
                    base = r * 4 * NS
                    mc = msk[:, r:r + 1]
                    imc = msk[:, NCORE + r:NCORE + r + 1]
                    src = pref[:, (pb0 + r) * NS: (pb0 + r + 1) * NS]
                    dst = pref[:, (pb0 + r + 1) * NS: (pb0 + r + 2) * NS]
                    B.op("vector", (lambda base, po, mc, imc: lambda e: e.tensor_scalar(
                        sel[:, NS:2 * NS], sums[:, base + po: base + po + NS],
                        mc, imc, op0=ALU.mult, op1=ALU.add))(base, po, mc, imc),
                        reads=["sums", "msk"], writes=["selp"])
                    B.op("vector", (lambda base, ho, mc: lambda e: e.tensor_scalar(
                        sel[:, 0:NS], sums[:, base + ho: base + ho + NS],
                        mc, None, op0=ALU.mult))(base, ho, mc),
                        reads=["sums", "msk"], writes=["selh"])
                    B.op("vector", (lambda dst, src: lambda e: e.tensor_mul(
                        dst, sel[:, NS:2 * NS], src))(dst, src),
                        reads=["selp", f"pref{pb0 + r}"], writes=[f"pref{pb0 + r + 1}"])
                    B.op("vector", (lambda dst: lambda e: e.tensor_add(
                        dst, dst, sel[:, 0:NS]))(dst),
                        reads=["selh", f"pref{pb0 + r + 1}"], writes=[f"pref{pb0 + r + 1}"])

        xnreads = [f"xn{kg}t{tc}" for kg in range(KG) for tc in range(NCH)]
        cbreads = [f"cb{kg}" for kg in range(KG)]

        TCS = (0, 2, 1, 3)

        def emit_chunk_loads(tci):
            tc = TCS[tci]
            par = tci % 2
            b = tc // 2
            for dirn, (hk, pk2, dst, dkey, prefkey, prefblk) in enumerate((
                    (("h", "f"), ("p", "f"), hfc, "hfc", "pref8", 8),
                    (("h", "b"), ("p", "b"), hbc, "hbc", "pref17", 17))):
                B.op("sync", (lambda hk, tc, par, dst: lambda e: e.dma_start(
                    out=dst[:, par * KG * CW:(par + 1) * KG * CW],
                    in_=pl[hk][tc * 128:(tc + 1) * 128, :]))(hk, tc, par, dst),
                    reads=[f"d_h{hk[1]}"],
                    writes=[f"{dkey}L{par}", f"{dkey}{par}"], dma=True)
                for kh in range(2):
                    B.op("sync", (lambda pk2, kh, tc: lambda e: e.dma_start(
                        out=stg[:, :],
                        in_=pl[pk2][tc * 128:(tc + 1) * 128,
                                    kh * 4 * CW:(kh + 1) * 4 * CW]))(pk2, kh, tc),
                        reads=[f"d_p{pk2[1]}"],
                        writes=["stg"], dma=True)
                    for kg in range(kh * 4, kh * 4 + 4):
                        pcol = pref[:, prefblk * NS + b * KG + kg: prefblk * NS + b * KG + kg + 1]
                        B.op("vector", (lambda dst, par, kg, pcol: lambda e: e.scalar_tensor_tensor(
                            dst[:, par * KG * CW + kg * CW: par * KG * CW + (kg + 1) * CW],
                            stg[:, (kg % 4) * CW:(kg % 4 + 1) * CW], pcol,
                            dst[:, par * KG * CW + kg * CW: par * KG * CW + (kg + 1) * CW],
                            op0=ALU.mult, op1=ALU.add))(dst, par, kg, pcol),
                            reads=["stg", f"{dkey}L{par}", prefkey],
                            writes=[f"{dkey}{par}"])

        wqB = ecb(nc.sbuf_tensor("wqB", [128, 2 * KG * D], BF16))
        ph2 = ExitStack()
        cst = ph2.enter_context(nc.sbuf_tensor("cst", [128, KG * PC], F32))
        gpl = ph2.enter_context(nc.sbuf_tensor("gpl", [128, KG * PC], BF16))
        gcm2 = ph2.enter_context(nc.sbuf_tensor("gcm2", [128, 2 * PC], F32))
        if True:
            WQ[1] = wqB
            load_pair(1)
            for m in range(KG):
                icol = bcs[:, 5 * KG + m: 5 * KG + m + 1]
                B.op("vector", (lambda m, icol: lambda e: e.tensor_scalar(
                    cst[:, m * PC:(m + 1) * PC], xnT[:, 0:PC], 0.0, icol,
                    op0=ALU.mult, op1=ALU.add))(m, icol),
                    reads=["bcs", "xn0t0", "xn0t1"], writes=[f"c{m}"])
                B.op("vector", (lambda m, icol: lambda e: e.tensor_scalar(
                    cbf[:, m * PC:(m + 1) * PC], xnT[:, 0:PC], 0.0, icol,
                    op0=ALU.mult, op1=ALU.add))(m, icol),
                    reads=["bcs", "xn0t0", "xn0t1"], writes=[f"cb{m}"])

            for t in range(2):
                for qi, q in enumerate(("f", "i", "c")):
                    li = t * 3 + qi
                    for mb in (0, 4):
                        first_blk = (li == 0 and mb == 0)
                        for m in range(mb, mb + 4):
                            slot = m % 4
                            for nh in range(2):
                                c0 = slot * PC + nh * CW
                                bk = [f"pb{c0 // CW}"]
                                for kg in range(KG):
                                    rds = ([wqkey(li, 0)] + xnreads
                                           if kg == 0 else
                                           ([wqkey(li, 0)] if kg == KG - 1 else []))
                                    xb = (["psk", "psu", "psk_b", "psu_s"]
                                          if first_blk and m == mb and nh == 0 and kg == 0
                                          else [])
                                    B.op("tensor", (lambda li, kg, m, t, nh, c0: lambda e: e.matmul(
                                        psg[:, c0:c0 + CW],
                                        ptile(li, kg, 0, m),
                                        xnT[:, kg * TOK + t * PC + nh * CW: kg * TOK + t * PC + (nh + 1) * CW],
                                        start=(kg == 0), stop=False))(li, kg, m, t, nh, c0),
                                        reads=rds, writes=bk + xb,
                                        fence=not first_blk)
                        for m in range(mb, mb + 4):
                            slot = m % 4
                            for nh in range(2):
                                c0 = slot * PC + nh * CW
                                bk = [f"pb{c0 // CW}"]
                                for kg in range(KG):
                                    rds = [f"cb{kg}"] + ([wqkey(li, 1)] if kg in (0, KG - 1) else [])
                                    B.op("tensor", (lambda li, kg, m, nh, c0: lambda e: e.matmul(
                                        psg[:, c0:c0 + CW],
                                        ptile(li, kg, 1, m),
                                        cbf[:, kg * PC + nh * CW: kg * PC + (nh + 1) * CW],
                                        start=False, stop=(kg == KG - 1)))(li, kg, m, nh, c0),
                                        reads=rds, writes=bk)
                            fn = AF.Tanh if q == "c" else AF.Sigmoid
                            bq = bcs[:, qi * KG + m: qi * KG + m + 1]
                            pbk = [f"pb{2 * slot}", f"pb{2 * slot + 1}"]
                            if q != "c":
                                B.op("scalar", (lambda m, slot, fn, bq: lambda e: e.activation(
                                    gpl[:, m * PC:(m + 1) * PC],
                                    psg[:, slot * PC:(slot + 1) * PC], fn, bias=bq))(m, slot, fn, bq),
                                    reads=pbk + ["bcs"], writes=[f"gpl{m}"])
                            else:
                                par2 = m % 2
                                gsl = gcm2[:, par2 * PC:(par2 + 1) * PC]
                                B.op("scalar", (lambda gsl, slot, bq: lambda e: e.activation(
                                    gsl, psg[:, slot * PC:(slot + 1) * PC],
                                    AF.Tanh, bias=bq))(gsl, slot, bq),
                                    reads=pbk + ["bcs"], writes=[f"gc{par2}"])
                                B.op("vector", (lambda gsl, m: lambda e: e.tensor_mul(
                                    gsl, gpl[:, m * PC:(m + 1) * PC], gsl))(gsl, m),
                                    reads=[f"gpl{m}", f"gc{par2}"], writes=[f"gc{par2}"])
                                B.op("vector", (lambda gsl, m: lambda e: e.tensor_add(
                                    cst[:, m * PC:(m + 1) * PC], cst[:, m * PC:(m + 1) * PC],
                                    gsl))(gsl, m),
                                    reads=[f"gc{par2}", f"c{m}"], writes=[f"c{m}"])
                    if q == "c":
                        for m in range(KG):
                            B.op("vector", (lambda m: lambda e: e.tensor_copy(
                                cbf[:, m * PC:(m + 1) * PC],
                                cst[:, m * PC:(m + 1) * PC]))(m),
                                reads=[f"c{m}"], writes=[f"cb{m}"])
                    if q == "f":
                        for m in range(KG):
                            B.op("vector", (lambda m: lambda e: e.tensor_mul(
                                cst[:, m * PC:(m + 1) * PC], cst[:, m * PC:(m + 1) * PC],
                                gpl[:, m * PC:(m + 1) * PC]))(m),
                                reads=[f"gpl{m}", f"c{m}"], writes=[f"c{m}"])
                    if li + 2 <= 6:
                        load_pair(li + 2)
                    if li == 0:
                        emit_fold()
                    elif li == 2:
                        emit_chunk_loads(0)
                    elif li == 4:
                        emit_chunk_loads(1)

            ffn_keys = ([f"gpl{m}" for m in range(KG)] + [f"c{m}" for m in range(KG)]
                        + [f"cb{m}" for m in range(KG)] + ["gc0", "gc1"])
            B.op("vector", lambda e: e.engine_nop(), reads=ffn_keys,
                 writes=["ffn_done"])

        ph2.close()
        xch = ecb(nc.sbuf_tensor("xch", [128, 2 * CW], F32))
        sch = ecb(nc.sbuf_tensor("sch", [128, KG * CW], F32))
        uoc = ecb(nc.sbuf_tensor("uoc", [128, KG * CW], F32))
        hf2 = ecb(nc.sbuf_tensor("hf2", [128, 2 * CW], F32))
        sg2 = ecb(nc.sbuf_tensor("sg2", [128, 2 * CW], F32))
        g2 = ecb(nc.sbuf_tensor("g2", [128, 2 * CW], F32))
        p2 = ecb(nc.sbuf_tensor("p2", [128, 2 * CW], F32))
        och = ecb(nc.sbuf_tensor("och", [128, 2 * CW], F32))
        if True:
            FD = ["ffn_done"]
            for kg in range(2 * KG):
                B.op("gpsimd", (lambda kg: lambda e: e.dma_start(
                    out=wqB[:, kg * D:(kg + 1) * D],
                    in_=w_out[kg * 128:(kg + 1) * 128, :]))(kg),
                    writes=[f"wout{kg}"] + (["wqb1"] if kg == 0 else []),
                    dma=True)

            for tci, tc in enumerate(TCS):
                par = tci % 2
                b = tc // 2
                ppos = tc - 2 * b
                if tci >= 1 and tci + 1 < NCH:
                    emit_chunk_loads(tci + 1)
                if tci % 2 == 0:
                    for m in range(KG):
                        slot = m % 4
                        fc0 = 4 * CW + slot * CW
                        for kg in range(KG):
                            rds = ["wqb0u"] + cbreads if kg == 0 else (["wqb0u"] if kg == KG - 1 else [])
                            B.op("tensor", (lambda kg, m, ppos, fc0: lambda e: e.matmul(
                                psg[:, fc0:fc0 + CW],
                                ptile(6, kg, 1, m),
                                cbf[:, kg * PC + ppos * CW: kg * PC + (ppos + 1) * CW],
                                start=(kg == 0), stop=(kg == KG - 1)))(kg, m, ppos, fc0),
                                reads=rds, writes=[f"pb{4 + slot}"])
                        B.op("scalar", (lambda m, fc0: lambda e: e.activation(
                            uoc[:, m * CW:(m + 1) * CW], psg[:, fc0:fc0 + CW],
                            AF.Copy))(m, fc0),
                            reads=FD + [f"pb{4 + (m % 4)}"], writes=[f"uoc{m}"])
                for m in range(KG):
                    slot = m % 4
                    for kgc in range(2 * KG):
                        if kgc < KG:
                            mv = hfc[:, par * KG * CW + kgc * CW: par * KG * CW + (kgc + 1) * CW]
                        else:
                            mv = hbc[:, par * KG * CW + (kgc - KG) * CW: par * KG * CW + (kgc - KG + 1) * CW]
                        rds = ([f"wout{kgc}"]
                               + ([f"hfc{par}", f"hbc{par}"]
                                  if kgc in (0, 2 * KG - 1) else []))
                        B.op("tensor", (lambda kgc, m, slot, mv: lambda e: e.matmul(
                            psg[:, slot * CW:(slot + 1) * CW],
                            wqB[:, kgc * D + m * 128: kgc * D + (m + 1) * 128],
                            mv, start=(kgc == 0), stop=(kgc == 2 * KG - 1)))(kgc, m, slot, mv),
                            reads=rds, writes=[f"pb{slot}"])
                    bocol = bcs[:, 4 * KG + m: 4 * KG + m + 1]
                    B.op("vector", (lambda m, slot, bocol: lambda e: e.tensor_scalar(
                        sch[:, m * CW:(m + 1) * CW], psg[:, slot * CW:(slot + 1) * CW],
                        bocol, None, op0=ALU.add))(m, slot, bocol),
                        reads=FD + [f"pb{slot}", "bcs"], writes=[f"sch{m}"])
                for m in range(KG):
                    slot = m % 4
                    mp = m % 2
                    fc0 = 4 * CW + slot * CW
                    B.op("scalar", (lambda m, tc, mp: lambda e: e.dma_start(
                        out=xch[:, mp * CW:(mp + 1) * CW],
                        in_=xT[m * 128:(m + 1) * 128, tc * CW:(tc + 1) * CW]))(m, tc, mp),
                        reads=FD, writes=[f"xch{mp}"], dma=True)
                    for kg in range(KG):
                        rds = ["wqb0w"] + xnreads if kg == 0 else (["wqb0w"] if kg == KG - 1 else [])
                        B.op("tensor", (lambda kg, m, tc, fc0: lambda e: e.matmul(
                            psg[:, fc0:fc0 + CW],
                            ptile(6, kg, 0, m),
                            xnT[:, kg * TOK + tc * CW: kg * TOK + (tc + 1) * CW],
                            start=(kg == 0), stop=(kg == KG - 1)))(kg, m, tc, fc0),
                            reads=rds, writes=[f"pb{4 + slot}"])
                    B.op("vector", (lambda mp, fc0, m: lambda e: e.tensor_add(
                        hf2[:, mp * CW:(mp + 1) * CW], psg[:, fc0:fc0 + CW],
                        uoc[:, m * CW:(m + 1) * CW]))(mp, fc0, m),
                        reads=FD + [f"pb{4 + slot}", f"uoc{m}"], writes=[f"hf2{mp}"])
                    bfcol = bcs[:, 3 * KG + m: 3 * KG + m + 1]
                    B.op("scalar", (lambda mp, bfcol: lambda e: e.activation(
                        hf2[:, mp * CW:(mp + 1) * CW], hf2[:, mp * CW:(mp + 1) * CW],
                        AF.Tanh, bias=bfcol))(mp, bfcol),
                        reads=[f"hf2{mp}", "bcs"], writes=[f"hf2{mp}"])
                    B.op("scalar", (lambda mp: lambda e: e.activation(
                        sg2[:, mp * CW:(mp + 1) * CW], hf2[:, mp * CW:(mp + 1) * CW],
                        AF.Sigmoid))(mp),
                        reads=FD + [f"hf2{mp}"], writes=[f"sg2{mp}"])
                    B.op("vector", (lambda mp: lambda e: e.tensor_mul(
                        g2[:, mp * CW:(mp + 1) * CW], hf2[:, mp * CW:(mp + 1) * CW],
                        sg2[:, mp * CW:(mp + 1) * CW]))(mp),
                        reads=FD + [f"hf2{mp}", f"sg2{mp}"], writes=[f"g2{mp}"])
                    B.op("vector", (lambda m, mp: lambda e: e.tensor_mul(
                        p2[:, mp * CW:(mp + 1) * CW], sch[:, m * CW:(m + 1) * CW],
                        g2[:, mp * CW:(mp + 1) * CW]))(m, mp),
                        reads=FD + [f"g2{mp}", f"sch{m}"], writes=[f"p2{mp}"])
                    oq = m % 2
                    B.op("vector", (lambda m, mp, oq: lambda e: e.tensor_add(
                        och[:, oq * CW:(oq + 1) * CW], p2[:, mp * CW:(mp + 1) * CW],
                        xch[:, mp * CW:(mp + 1) * CW]))(m, mp, oq),
                        reads=FD + [f"p2{mp}", f"xch{mp}"], writes=[f"och{oq}"])
                    B.op("sync", (lambda m, tc, oq: lambda e: e.dma_start(
                        out=o_out[m * 128:(m + 1) * 128, tc * CW:(tc + 1) * CW],
                        in_=och[:, oq * CW:(oq + 1) * CW]))(m, tc, oq),
                        reads=[f"och{oq}"], dma=True, force=True)

        B.finalize(block, semh)
        stB.close()
    return nc



def _bias_cols(*vecs):
    cols = [np.ascontiguousarray(np.asarray(v, np.float32).reshape(KG, 128).T)
            for v in vecs]
    return np.ascontiguousarray(np.concatenate(cols, axis=1))


def kernel(**inputs):
    f32 = np.float32
    x = np.asarray(inputs["L0_attn_x"], f32)
    g = lambda k: np.asarray(inputs[k], f32)

    xT_shards = []
    for c in range(NCORE):
        sl = x[:, c * PC:(c + 1) * PC, :]
        xT_shards.append(np.ascontiguousarray(sl.transpose(2, 0, 1).reshape(D, TOK)))

    bcols_A = _bias_cols(g("L0_attn_bz_f"), g("L0_attn_bh_f"),
                         g("L0_attn_bz_b"), g("L0_attn_bh_b"))
    wA = {nm: np.ascontiguousarray(g(src)) for nm, src in
          (("wz_f", "L0_attn_Wz_f"), ("wh_f", "L0_attn_Wh_f"),
           ("wz_b", "L0_attn_Wz_b"), ("wh_b", "L0_attn_Wh_b"))}
    bcols_B = _bias_cols(g("L0_ffn_bf"), g("L0_ffn_bi"), g("L0_ffn_bc"),
                         g("L0_ffn_bo"), g("L0_attn_b_out"), g("L0_ffn_in_c"))
    w_out = np.ascontiguousarray(g("L0_attn_W_out"))
    fwt = {nm + "T": np.ascontiguousarray(g("L0_ffn_" + nm).T)
           for nm in ("wf", "uf", "wi", "ui", "wc", "uc", "wo", "uo")}

    masks = []
    for c in range(NCORE):
        m = np.zeros((128, 2 * NCORE), f32)
        m[:, :c] = 1.0
        m[:, NCORE:NCORE + NCORE] = 1.0 - m[:, :NCORE]
        masks.append(np.ascontiguousarray(m))

    nc = build_fused()
    in_maps = [dict(xT=xT_shards[c], xTm=xT_shards[NCORE - 1 - c],
                    bcolsA=bcols_A, bcolsB=bcols_B, maskc=masks[c],
                    w_out=w_out, **wA, **fwt)
               for c in range(NCORE)]
    res = run_bass_kernel_spmd(nc, in_maps, core_ids=list(range(NCORE)))

    out = np.empty((NB, S, D), f32)
    for c in range(NCORE):
        oc = np.asarray(res.results[c]["o_out"])
        out[:, c * PC:(c + 1) * PC, :] = oc.reshape(D, NB, PC).transpose(1, 2, 0)
    return out
